# version 01 (final)
# baseline (speedup 1.0000x reference)
import numpy as np
import ml_dtypes
from contextlib import ExitStack
import concourse.bass as bass
import concourse.mybir as mybir
from concourse.bass_utils import run_bass_kernel_spmd

F32 = mybir.dt.float32
BF16 = mybir.dt.bfloat16
I32 = mybir.dt.int32
AF = mybir.ActivationFunctionType
ALU = mybir.AluOpType
AX = mybir.AxisListType

BATCH = 2
SEQ = 4096
D = 2048
NT = SEQ // 128
NQ = 8
DI = D // 128
M0 = 16.0
M1 = 14.0
GELU = AF.Gelu_apprx_tanh
LAMBDA_INIT = 0.8 - 0.6 * float(np.exp(-0.3 * 12))

ENGINES = ("sync", "gpsimd", "scalar", "vector", "tensor")


NSLOT = 8


class Prog:

    COMPUTE = ("scalar", "vector", "tensor")
    QUEUES = ("sync", "gpsimd")

    def __init__(self, nc, mk_sem):
        self.nc = nc
        self.ops = []
        self.last_w = {}
        self.readers = {}
        self.csem = {e: mk_sem(f"c_{e}") for e in self.COMPUTE}
        self.ccount = {e: 0 for e in self.COMPUTE}
        self.qsem = {q: [mk_sem(f"q_{q}{i}") for i in range(NSLOT)] for q in self.QUEUES}
        self.qcount = {q: 0 for q in self.QUEUES}

    SELF_SYNC = ("scalar", "vector")

    def _dep_waits(self, engine, reads, writes):
        waits = {}

        def need(tok):
            owner, sem, val = tok
            if owner == engine and owner not in self.SELF_SYNC:
                return
            if owner == engine and val >= self.ccount.get(engine, 0) + 1:
                pass
            if waits.get(sem, 0) < val:
                waits[sem] = val

        for r in reads:
            lw = self.last_w.get(id(r))
            if lw is not None:
                need(lw)
        for w in writes:
            lw = self.last_w.get(id(w))
            if lw is not None:
                need(lw)
            for rd in self.readers.get(id(w), ()):
                need(rd)
        return waits

    def _record(self, tok, reads, writes):
        for w in writes:
            self.last_w[id(w)] = tok
            self.readers[id(w)] = []
        for r in reads:
            self.readers.setdefault(id(r), []).append(tok)

    def op(self, engine, fn, reads=(), writes=()):
        assert engine in self.COMPUTE
        waits = self._dep_waits(engine, reads, writes)
        self.ccount[engine] += 1
        tok = (engine, self.csem[engine], self.ccount[engine])
        self.ops.append((engine, fn, waits, (self.csem[engine], 1)))
        self._record(tok, reads, writes)

    def dma(self, queue, dst, src, reads=(), writes=()):
        assert queue in self.QUEUES
        waits = self._dep_waits(queue, reads, writes)
        i = self.qcount[queue]
        self.qcount[queue] = i + 1
        sem = self.qsem[queue][i % NSLOT]
        prev = 16 * (i // NSLOT)
        if prev > 0 and waits.get(sem, 0) < prev:
            waits[sem] = prev
        self.ops.append((queue, lambda e: e.dma_start(dst, src), waits, (sem, 16)))
        tok = (queue, sem, 16 * (i // NSLOT + 1))
        self._record(tok, reads, writes)

    def emit(self, block):
        hiw = {}

        def mk(engine):
            def body(eng):
                h = hiw.setdefault(engine, {})
                for op_eng, fn, waits, inc in self.ops:
                    if op_eng != engine:
                        continue
                    for sem, val in waits.items():
                        if h.get(id(sem), 0) < val:
                            eng.wait_ge(sem, val)
                            h[id(sem)] = val
                    ins = fn(eng)
                    if inc is not None:
                        ins.then_inc(*inc)
                if engine == "sync":
                    for e2 in self.COMPUTE:
                        if self.ccount[e2] > 0:
                            eng.wait_ge(self.csem[e2], self.ccount[e2])
                    for q in self.QUEUES:
                        n = self.qcount[q]
                        for s in range(NSLOT):
                            cnt = 16 * ((n - 1 - s) // NSLOT + 1) if n > s else 0
                            if cnt > 0:
                                eng.wait_ge(self.qsem[q][s], cnt)
            return body

        block.sync(mk("sync"))
        block.gpsimd(mk("gpsimd"))
        block.scalar(mk("scalar"))
        block.vector(mk("vector"))
        block.tensor(mk("tensor"))


def _run(nc, in_maps):
    res = run_bass_kernel_spmd(nc, in_maps, core_ids=list(range(len(in_maps))))
    return res.results



def ln_tile(p, xt, xn_bf, tmp):
    s1, sq, s2, mean, var, rstd, nmr = (tmp[k] for k in
                                        ("s1", "sq", "s2", "mean", "var", "rstd", "nmr"))
    p.op("vector", lambda e: e.reduce_sum(s1[:], xt[:], AX.X), reads=[xt], writes=[s1])
    p.op("scalar", lambda e: e.activation(sq[:], xt[:], AF.Square), reads=[xt], writes=[sq])
    p.op("vector", lambda e: e.reduce_sum(s2[:], sq[:], AX.X), reads=[sq], writes=[s2])
    p.op("vector", lambda e: e.tensor_scalar(mean[:], s1[:], 1.0 / D, None, ALU.mult),
         reads=[s1], writes=[mean])
    p.op("vector", lambda e: e.tensor_scalar(var[:], s2[:], 1.0 / D, None, ALU.mult),
         reads=[s2], writes=[var])
    p.op("vector", lambda e: e.scalar_tensor_tensor(nmr[:], mean[:], 1.0, mean[:], ALU.mult, ALU.mult),
         reads=[mean], writes=[nmr])
    p.op("vector", lambda e: e.tensor_sub(var[:], var[:], nmr[:]), reads=[var, nmr], writes=[var])
    p.op("vector", lambda e: e.tensor_scalar(var[:], var[:], 1e-5, None, ALU.add),
         reads=[var], writes=[var])
    p.op("vector", lambda e: e.reciprocal(rstd[:], var[:]), reads=[var], writes=[rstd])
    p.op("scalar", lambda e: e.activation(rstd[:], rstd[:], AF.Sqrt), reads=[rstd], writes=[rstd])
    p.op("vector", lambda e: e.scalar_tensor_tensor(nmr[:], mean[:], -1.0, rstd[:], ALU.mult, ALU.mult),
         reads=[mean, rstd], writes=[nmr])
    p.op("scalar", lambda e: e.activation(xn_bf[:], xt[:], AF.Identity, bias=nmr[:], scale=rstd[:]),
         reads=[xt, nmr, rstd], writes=[xn_bf])


def mk_ln_tmp(stack, nc, tag, sq=None):
    names = ("s1", "s2", "mean", "var", "rstd", "nmr")
    d = {k: stack.enter_context(nc.sbuf_tensor(f"{tag}_{k}", [128, 1], F32)) for k in names}
    d["sq"] = sq if sq is not None else stack.enter_context(
        nc.sbuf_tensor(f"{tag}_sq", [128, D], F32))
    return d



def build_l0a():
    nc = bass.Bass("TRN2", target_bir_lowering=False, debug=False)
    xb = nc.dram_tensor("xb", [SEQ, D], F32, kind="ExternalInput").ap()
    xq = nc.dram_tensor("xq", [NQ * 128, D], F32, kind="ExternalInput").ap()
    wqkv = nc.dram_tensor("wqkv", [16, 3, DI, 128, 128], F32, kind="ExternalInput").ap()
    bqkv = nc.dram_tensor("bqkv", [16, 3, 128, 1], F32, kind="ExternalInput").ap()
    wo = nc.dram_tensor("wo", [DI, 128, D], F32, kind="ExternalInput").ap()
    bo = nc.dram_tensor("bo", [1, D], F32, kind="ExternalInput").ap()
    masks = nc.dram_tensor("masks", [128, 4, 128], BF16, kind="ExternalInput").ap()
    ident = nc.dram_tensor("ident", [128, 128], BF16, kind="ExternalInput").ap()
    y = nc.dram_tensor("y", [NQ * 128, D], F32, kind="ExternalOutput").ap()
    attn_dr = nc.dram_tensor("attn_dr", [16, 128, NQ * 128], BF16, kind="Internal").ap()
    xnTo_dr = nc.dram_tensor("xnTo_dr", [DI, 128, NQ * 128], BF16, kind="Internal").ap()

    with ExitStack() as st:
        sb = lambda n, s, dt=F32: st.enter_context(nc.sbuf_tensor(n, s, dt))
        ps = lambda n, s, dt=F32: st.enter_context(nc.psum_tensor(n, s, dt))
        block = st.enter_context(nc.Block())
        p = Prog(nc, lambda n: st.enter_context(nc.semaphore(n)))

        xnT = sb("xnT", [128, DI, SEQ], BF16)
        tstg = sb("tstg", [128, 128], BF16)
        id_sb = sb("id_sb", [128, 128], BF16)
        mask_sb = sb("mask_sb", [128, 4, 128], BF16)
        xt = sb("xt", [128, D])
        xnbf = sb("xnbf", [128, D], BF16)
        yv = sb("yv", [128, D])
        tmp = mk_ln_tmp(st, nc, "ln", sq=yv)
        ptp = ps("ptp", [128, 128], BF16)

        p.dma("sync", id_sb[:], ident, writes=[id_sb])
        p.dma("sync", mask_sb[:], masks, writes=[mask_sb])

        for isq in (0, 1):
            ntile = NQ if isq else NT
            for kt in range(ntile):
                p.dma("sync", xt[:], (xq if isq else xb)[kt * 128:(kt + 1) * 128, :], writes=[xt])
                ln_tile(p, xt, xnbf, tmp)
                for di in range(DI):
                    p.op("tensor", lambda e, di=di: e.transpose(
                        ptp[:], xnbf[:, di * 128:(di + 1) * 128], id_sb[:]),
                        reads=[xnbf, id_sb], writes=[ptp])
                    if isq:
                        p.op("vector", lambda e: e.tensor_copy(tstg[:], ptp[:]),
                             reads=[ptp], writes=[tstg])
                        p.dma("gpsimd", xnTo_dr[di, :, kt * 128:(kt + 1) * 128], tstg[:],
                              reads=[tstg], writes=[xnTo_dr])
                    else:
                        p.op("vector", lambda e, di=di, kt=kt: e.tensor_copy(
                            xnT[:, di, kt * 128:(kt + 1) * 128], ptp[:]),
                            reads=[ptp], writes=[xnT])

        wst = sb("wst", [128, 128])
        wbf = sb("wbf", [128, DI, 128], BF16)
        bsec = sb("bsec", [128, 1])
        kT = sb("kT", [128, SEQ], BF16)
        vN = sb("vN", [128, NT, 128], BF16)
        qT = sb("qT", [128, NQ * 128], BF16)
        vtb = sb("vtb", [128, 512], BF16)
        pt4 = sb("pt4", [128, 512], BF16)
        sumr = sb("sumr", [1, 128])
        rbc = sb("rbc", [128, 128])
        aoT = sb("aoT", [128, 128], BF16)
        onesc = sb("onesc", [128, 1], BF16)
        onesr = sb("onesr", [1, 128])
        qx = sb("qx", [128, DI, 128], BF16)
        m0c = sb("m0c", [128, 1])
        p.op("vector", lambda e: e.memset(onesc[:], 1.0), writes=[onesc])
        p.op("vector", lambda e: e.memset(onesr[:], 1.0), writes=[onesr])
        p.op("vector", lambda e: e.memset(m0c[:], -M0), writes=[m0c])

        pk = ps("pk", [128, 512])
        pq = ps("pq", [128, 128])
        psc = ps("psc", [128, 512])
        pav = ps("pav", [128, 128])
        psum_s = ps("psum_s", [1, 128])
        prb = ps("prb", [128, 128])

        def load_sec(h, sec):
            for di in range(DI):
                p.dma("sync", wst[:], wqkv[h, sec, di], writes=[wst])
                p.op("scalar", lambda e, di=di: e.copy(wbf[:, di, :], wst[:]),
                     reads=[wst], writes=[wbf])
            p.dma("sync", bsec[:], bqkv[h, sec], writes=[bsec])

        for h in range(16):
            load_sec(h, 1)
            for ch in range(SEQ // 512):
                for di in range(DI):
                    p.op("tensor", lambda e, ch=ch, di=di: e.matmul(
                        pk[:], wbf[:, di, :], xnT[:, di, ch * 512:(ch + 1) * 512],
                        start=(di == 0), stop=(di == DI - 1)),
                        reads=[wbf, xnT], writes=[pk])
                p.op("scalar", lambda e, ch=ch: e.activation(
                    kT[:, ch * 512:(ch + 1) * 512], pk[:], AF.Identity, bias=bsec[:]),
                    reads=[pk, bsec], writes=[kT])
            load_sec(h, 2)
            for ch in range(SEQ // 512):
                for di in range(DI):
                    p.op("tensor", lambda e, ch=ch, di=di: e.matmul(
                        pk[:], wbf[:, di, :], xnT[:, di, ch * 512:(ch + 1) * 512],
                        start=(di == 0), stop=(di == DI - 1)),
                        reads=[wbf, xnT], writes=[pk])
                p.op("scalar", lambda e, ch=ch: e.activation(
                    vtb[:], pk[:], AF.Identity, bias=bsec[:]),
                    reads=[pk, bsec], writes=[vtb])
                for i in range(4):
                    kt = ch * 4 + i
                    p.op("tensor", lambda e, i=i: e.transpose(
                        ptp[:], vtb[:, i * 128:(i + 1) * 128], id_sb[:]),
                        reads=[vtb, id_sb], writes=[ptp])
                    p.op("vector", lambda e, kt=kt: e.tensor_copy(vN[:, kt, :], ptp[:]),
                         reads=[ptp], writes=[vN])
            load_sec(h, 0)
            for j in range(NQ):
                for di in range(DI):
                    p.dma("sync", qx[:, di, :], xnTo_dr[di, :, j * 128:(j + 1) * 128],
                          reads=[xnTo_dr], writes=[qx])
                for di in range(DI):
                    p.op("tensor", lambda e, j=j, di=di: e.matmul(
                        pq[:], wbf[:, di, :], qx[:, di, :],
                        start=(di == 0), stop=(di == DI - 1)),
                        reads=[wbf, qx], writes=[pq])
                p.op("scalar", lambda e, j=j: e.activation(
                    qT[:, j * 128:(j + 1) * 128], pq[:], AF.Identity, bias=bsec[:]),
                    reads=[pq, bsec], writes=[qT])
            for j in range(NQ):
                ng = j + 1
                nacc = 4 * ng
                acc = 0
                for g in range(ng):
                    for i in range(4):
                        kt = g * 4 + i
                        p.op("tensor", lambda e, kt=kt, i=i, j=j: e.matmul(
                            psc[:, i * 128:(i + 1) * 128],
                            kT[:, kt * 128:(kt + 1) * 128],
                            qT[:, j * 128:(j + 1) * 128], start=True, stop=True),
                            reads=[kT, qT], writes=[psc])
                    p.op("scalar", lambda e: e.activation(
                        pt4[:], psc[:], AF.Exp, bias=m0c[:], scale=1.0 / np.sqrt(128.0)),
                        reads=[psc, m0c], writes=[pt4])
                    if g == ng - 1:
                        p.op("vector", lambda e: e.tensor_mul(
                            pt4[:], pt4[:], mask_sb[:].rearrange("p a b -> p (a b)")),
                            reads=[pt4, mask_sb], writes=[pt4])
                    for i in range(4):
                        kt = g * 4 + i
                        acc += 1
                        p.op("tensor", lambda e, i=i, acc=acc: e.matmul(
                            psum_s[:], onesc[:], pt4[:, i * 128:(i + 1) * 128],
                            start=(acc == 1), stop=(acc == nacc)),
                            reads=[onesc, pt4], writes=[psum_s])
                        p.op("tensor", lambda e, kt=kt, i=i, acc=acc: e.matmul(
                            pav[:], vN[:, kt, :], pt4[:, i * 128:(i + 1) * 128],
                            start=(acc == 1), stop=(acc == nacc)),
                            reads=[vN, pt4], writes=[pav])
                p.op("vector", lambda e: e.tensor_copy(sumr[:], psum_s[:]),
                     reads=[psum_s], writes=[sumr])
                p.op("vector", lambda e: e.reciprocal(sumr[:], sumr[:]),
                     reads=[sumr], writes=[sumr])
                p.op("tensor", lambda e: e.matmul(prb[:], onesr[:], sumr[:],
                                                  start=True, stop=True),
                     reads=[onesr, sumr], writes=[prb])
                p.op("vector", lambda e: e.tensor_copy(rbc[:], prb[:]), reads=[prb], writes=[rbc])
                p.op("vector", lambda e: e.tensor_mul(aoT[:], pav[:], rbc[:]),
                     reads=[pav, rbc], writes=[aoT])
                p.dma("gpsimd", attn_dr[h, :, j * 128:(j + 1) * 128], aoT[:],
                      reads=[aoT], writes=[attn_dr])

        at_sb = wbf
        wos = sb("wos", [128, 512])
        wob1 = sb("wob1", [128, 512], BF16)
        bo_sb = sb("bo_sb", [1, D], BF16)
        ones1b = sb("ones1b", [1, 128], BF16)
        xq_sb = xt
        pop = ps("pop", [128, 512])
        p.dma("sync", xt[0:1, :], bo, writes=[xt])
        p.op("scalar", lambda e: e.copy(bo_sb[:], xt[0:1, :]), reads=[xt], writes=[bo_sb])
        p.op("vector", lambda e: e.memset(ones1b[:], 1.0), writes=[ones1b])
        for j in range(NQ):
            for di in range(DI):
                p.dma("sync", at_sb[:, di, :], attn_dr[di, :, j * 128:(j + 1) * 128],
                      reads=[attn_dr], writes=[at_sb])
            p.dma("sync", xq_sb[:], xq[j * 128:(j + 1) * 128, :], writes=[xq_sb])
            for dc in range(4):
                for di in range(DI):
                    p.dma("sync", wos[:], wo[di, :, dc * 512:(dc + 1) * 512], writes=[wos])
                    p.op("scalar", lambda e: e.copy(wob1[:], wos[:]), reads=[wos], writes=[wob1])
                    p.op("tensor", lambda e, dc=dc, di=di: e.matmul(
                        pop[:], at_sb[:, di, :], wob1[:],
                        start=(di == 0), stop=False),
                        reads=[at_sb, wob1], writes=[pop])
                p.op("tensor", lambda e, dc=dc: e.matmul(
                    pop[:], ones1b[:], bo_sb[:, dc * 512:(dc + 1) * 512],
                    start=False, stop=True),
                    reads=[ones1b, bo_sb], writes=[pop])
                p.op("vector", lambda e, dc=dc: e.tensor_add(
                    yv[:, dc * 512:(dc + 1) * 512], pop[:],
                    xq_sb[:, dc * 512:(dc + 1) * 512]),
                    reads=[pop, xq_sb], writes=[yv])
            p.dma("gpsimd", y[j * 128:(j + 1) * 128, :], yv[:], reads=[yv])

        p.emit(block)
    return nc



PACKMAP = [(0, 0), (0, 32), (0, 64), (1, 0), (1, 32), (1, 64), (2, 0), (2, 32)]
NE = 8
NF = 64


def build_l0f(phases=99, dbg=False):
    nc = bass.Bass("TRN2", target_bir_lowering=False, debug=False)
    xc = nc.dram_tensor("xc", [NQ * 128, D], F32, kind="ExternalInput").ap()
    wr = nc.dram_tensor("wr", [128, DI, NE], F32, kind="ExternalInput").ap()
    br = nc.dram_tensor("br", [1, NE], F32, kind="ExternalInput").ap()
    w1 = nc.dram_tensor("w1", [NF, DI, 128, 128], F32, kind="ExternalInput").ap()
    b1c = nc.dram_tensor("b1c", [NF, 128, 1], F32, kind="ExternalInput").ap()
    w2 = nc.dram_tensor("w2", [DI, NF, 128, 128], F32, kind="ExternalInput").ap()
    b2r = nc.dram_tensor("b2r", [1, D], F32, kind="ExternalInput").ap()
    a1t = nc.dram_tensor("a1t", [NE, 128, DI, 16], F32, kind="ExternalInput").ap()
    b1tp = nc.dram_tensor("b1tp", [NF, NE, 16, 128], F32, kind="ExternalInput").ap()
    a2t = nc.dram_tensor("a2t", [NE, 128, NF, 16], F32, kind="ExternalInput").ap()
    b2tp = nc.dram_tensor("b2tp", [NE, 16, D], F32, kind="ExternalInput").ap()
    ident = nc.dram_tensor("ident", [128, 128], BF16, kind="ExternalInput").ap()
    y = nc.dram_tensor("y", [NQ * 128, D], F32, kind="ExternalOutput").ap()
    if dbg:
        dbg_base = nc.dram_tensor("dbg_base", [128, 512], BF16, kind="ExternalOutput").ap()
        dbg_g = nc.dram_tensor("dbg_g", [128, 512], BF16, kind="ExternalOutput").ap()
        dbg_ca = nc.dram_tensor("dbg_ca", [128, 512], BF16, kind="ExternalOutput").ap()
        dbg_comb = nc.dram_tensor("dbg_comb", [8, 512], BF16, kind="ExternalOutput").ap()
        dbg_t1 = nc.dram_tensor("dbg_t1", [16, NE, 512], BF16, kind="ExternalOutput").ap()
        dbg_t2 = nc.dram_tensor("dbg_t2", [16, NE, 512], BF16, kind="ExternalOutput").ap()
        dbg_out = nc.dram_tensor("dbg_out", [128, 512], BF16, kind="ExternalOutput").ap()
        dbg_mix = nc.dram_tensor("dbg_mix", [128, 512], BF16, kind="ExternalOutput").ap()
        dbg_probs = nc.dram_tensor("dbg_probs", [128, NE], F32, kind="ExternalOutput").ap()
        dbg_cmbf = nc.dram_tensor("dbg_cmbf", [128, NE], F32, kind="ExternalOutput").ap()
        dbg_lg = nc.dram_tensor("dbg_lg", [128, NE], F32, kind="ExternalOutput").ap()
        dbg_xt = nc.dram_tensor("dbg_xt", [128, D], F32, kind="ExternalOutput").ap()
        dbg_xn = nc.dram_tensor("dbg_xn", [128, D], BF16, kind="ExternalOutput").ap()
        dbg_st = nc.dram_tensor("dbg_st", [6, 128, 1], F32, kind="ExternalOutput").ap()

    with ExitStack() as st:
        sb = lambda n, s, dt=F32: st.enter_context(nc.sbuf_tensor(n, s, dt))
        ps = lambda n, s, dt=F32: st.enter_context(nc.psum_tensor(n, s, dt))
        block = st.enter_context(nc.Block())
        p = Prog(nc, lambda n: st.enter_context(nc.semaphore(n)))

        id_sb = sb("id_sb", [128, 128], BF16)
        xt = sb("xt", [128, D])
        xnbf = sb("xnbf", [128, D], BF16)
        yv = sb("yv", [128, D])
        tmp = mk_ln_tmp(st, nc, "ln", sq=yv)
        xnT = sb("xnT", [128, DI, 512], BF16)
        mixT = sb("mixT", [128, NF, 512], BF16)
        outT = sb("outT", [128, DI, 512], BF16)
        t1T = sb("t1T", [16, NE, 512], BF16)
        a2_sb = sb("a2_sb", [128, NE, NF, 16], BF16)
        a2st = sb("a2st", [128, NF, 16])
        a1f = sb("a1f", [128, DI, 16])
        a1b = sb("a1b", [128, DI, 16], BF16)
        wrf = sb("wrf", [128, DI, NE])
        wrb = sb("wrb", [128, DI, NE], BF16)
        brf = sb("brf", [1, NE])
        brb = sb("brb", [1, NE], BF16)
        w1st = [sb(f"w1st{i}", [128, 128]) for i in range(2)]
        w1b = [sb(f"w1b{i}", [128, 128], BF16) for i in range(2)]
        w2st = [sb(f"w2st{i}", [128, 128]) for i in range(2)]
        w2b = [sb(f"w2b{i}", [128, 128], BF16) for i in range(2)]
        b1p = [sb(f"b1p{i}", [16, 128]) for i in range(2)]
        b1eb = sb("b1eb", [16, NE, 128], BF16)
        b2tb = sb("b2tb", [16, NE, 128], BF16)
        b2st = [sb(f"b2st{i}", [16, 128]) for i in range(2)]
        b2bf = sb("b2bf", [1, D], BF16)
        b1col = sb("b1col", [128, 1])
        bc_sb = [sb(f"bc{e}", [128, 512], BF16) for e in range(NE)]
        baseT = sb("baseT", [128, 512], BF16)
        g_sb = sb("g_sb", [128, 512], BF16)
        ca_sb = sb("ca_sb", [128, 512], BF16)
        t2a = sb("t2a", [16, NE, 512])
        t2ab = sb("t2ab", [16, NE, 512], BF16)
        combT = sb("combT", [8, 512], BF16)
        crow = sb("crow", [1, 512], BF16)
        csr = sb("csr", [1, 512], BF16)
        lg = sb("lg", [128, NE])
        pe_sb = sb("pe_sb", [128, NE])
        probs = sb("probs", [128, NE])
        cmbf = sb("cmbf", [128, NE])
        cmbb = sb("cmbb", [128, NE], BF16)
        mrow = sb("mrow", [128, 1])
        negm = sb("negm", [128, 1])
        sume = sb("sume", [128, 1])
        rs = sb("rs", [128, 1])
        mx2 = sb("mx2", [128, 1])
        msk = sb("msk", [128, NE])
        tcp = sb("tcp", [128, 128])
        ones1b = sb("ones1b", [1, 128], BF16)
        ones8b = sb("ones8b", [8, 1], BF16)

        P0 = ps("P0", [128, 512])
        P1 = ps("P1", [128, 512])
        P2 = ps("P2", [128, 512])
        P3 = ps("P3", [128, 512])
        PB0 = ps("PB0", [128, 256], BF16)

        p.dma("sync", id_sb[:], ident, writes=[id_sb])
        p.op("vector", lambda e: e.memset(ones1b[:], 1.0), writes=[ones1b])
        p.op("vector", lambda e: e.memset(ones8b[:], 1.0), writes=[ones8b])
        for e in range(NE):
            p.dma("sync", a2st[:], a2t[e], writes=[a2st])
            p.op("scalar", lambda e_, e=e: e_.copy(a2_sb[:, e, :, :], a2st[:]),
                reads=[a2st], writes=[a2_sb])
        p.dma("sync", wrf[:], wr, writes=[wrf])
        p.op("scalar", lambda e: e.copy(wrb[:], wrf[:]), reads=[wrf], writes=[wrb])
        p.dma("sync", brf[:], br, writes=[brf])
        p.op("scalar", lambda e: e.copy(brb[:], brf[:]), reads=[brf], writes=[brb])

        p.dma("sync", xt[0:1, :], b2r, writes=[xt])
        p.op("scalar", lambda e: e.copy(b2bf[:], xt[0:1, :]), reads=[xt], writes=[b2bf])

        for th in range(2):
            t0 = th * 512
            for tt in range(4):
                p.dma("sync", xt[:], xc[t0 + tt * 128:t0 + (tt + 1) * 128, :], writes=[xt])
                if dbg and th == 0 and tt == 0:
                    p.dma("gpsimd", dbg_xt, xt[:], reads=[xt])
                ln_tile(p, xt, xnbf, tmp)
                if dbg and th == 0 and tt == 0:
                    p.dma("gpsimd", dbg_xn, xnbf[:], reads=[xnbf])
                    for i_, k_ in enumerate(("s1", "s2", "mean", "var", "rstd", "nmr")):
                        p.dma("gpsimd", dbg_st[i_], tmp[k_][:], reads=[tmp[k_]])
                for di in range(DI):
                    p.op("tensor", lambda e, di=di: e.transpose(
                        PB0[:, 0:128], xnbf[:, di * 128:(di + 1) * 128], id_sb[:]),
                        reads=[xnbf, id_sb], writes=[PB0])
                    p.op("vector", lambda e, di=di, tt=tt: e.tensor_copy(
                        xnT[:, di, tt * 128:(tt + 1) * 128], PB0[:, 0:128]),
                        reads=[PB0], writes=[xnT])
            if phases < 2:
                continue
            for tt in range(4):
                for di in range(DI):
                    p.op("tensor", lambda e, tt=tt, di=di: e.matmul(
                        P0[:, 0:NE], xnT[:, di, tt * 128:(tt + 1) * 128], wrb[:, di, :],
                        start=(di == 0), stop=False),
                        reads=[xnT, wrb], writes=[P0])
                p.op("tensor", lambda e: e.matmul(P0[:, 0:NE], ones1b[:], brb[:],
                                                  start=False, stop=True),
                     reads=[ones1b, brb], writes=[P0])
                p.op("vector", lambda e: e.tensor_copy(lg[:], P0[:, 0:NE]),
                     reads=[P0], writes=[lg])
                p.op("vector", lambda e: e.reduce_max(mrow[:], lg[:], AX.X),
                     reads=[lg], writes=[mrow])
                p.op("vector", lambda e: e.tensor_scalar(negm[:], mrow[:], -1.0, None, ALU.mult),
                     reads=[mrow], writes=[negm])
                p.op("scalar", lambda e: e.activation(pe_sb[:], lg[:], AF.Exp,
                                                      bias=negm[:], accum_out=sume[:]),
                     reads=[lg, negm], writes=[pe_sb, sume])
                p.op("vector", lambda e: e.reciprocal(rs[:], sume[:]), reads=[sume], writes=[rs])
                p.op("vector", lambda e: e.tensor_scalar(probs[:], pe_sb[:], rs[:], None, ALU.mult),
                     reads=[pe_sb, rs], writes=[probs])
                p.op("vector", lambda e: e.reduce_max(mrow[:], probs[:], AX.X),
                     reads=[probs], writes=[mrow])
                p.op("vector", lambda e: e.tensor_scalar(msk[:], probs[:], mrow[:], None, ALU.is_ge),
                     reads=[probs, mrow], writes=[msk])
                p.op("vector", lambda e: e.scalar_tensor_tensor(msk[:], msk[:], -1e9, probs[:],
                                                                ALU.mult, ALU.add),
                     reads=[msk, probs], writes=[msk])
                p.op("vector", lambda e: e.reduce_max(mx2[:], msk[:], AX.X),
                     reads=[msk], writes=[mx2])
                p.op("vector", lambda e: e.tensor_scalar(msk[:], probs[:], mx2[:], None, ALU.is_ge),
                     reads=[probs, mx2], writes=[msk])
                p.op("vector", lambda e: e.tensor_mul(cmbf[:], probs[:], msk[:]),
                     reads=[probs, msk], writes=[cmbf])
                if dbg and th == 0 and tt == 0:
                    p.dma("gpsimd", dbg_probs, probs[:], reads=[probs])
                    p.dma("gpsimd", dbg_cmbf, cmbf[:], reads=[cmbf])
                    p.dma("gpsimd", dbg_lg, lg[:], reads=[lg])
                p.op("scalar", lambda e: e.copy(cmbb[:], cmbf[:]), reads=[cmbf], writes=[cmbb])
                p.op("tensor", lambda e: e.transpose(PB0[0:NE, 0:128], cmbb[:], id_sb[:]),
                     reads=[cmbb, id_sb], writes=[PB0])
                p.op("vector", lambda e, tt=tt: e.tensor_copy(
                    combT[:, tt * 128:(tt + 1) * 128], PB0[0:NE, 0:128]),
                    reads=[PB0], writes=[combT])
            p.op("tensor", lambda e: e.matmul(P2[0:1, :], ones8b[:], combT[:],
                                              start=True, stop=True),
                 reads=[ones8b, combT], writes=[P2])
            p.op("scalar", lambda e: e.copy(csr[:], P2[0:1, :]), reads=[P2], writes=[csr])
            if dbg and th == 0:
                p.dma("gpsimd", dbg_comb, combT[:], reads=[combT])
            for e in range(NE):
                p.dma("sync", crow[:], combT[e:e + 1, :], reads=[combT], writes=[crow])
                p.op("tensor", lambda e_, e=e: e_.matmul(P1[:], ones1b[:], crow[:],
                                                         start=True, stop=True),
                     reads=[ones1b, crow], writes=[P1])
                p.op("scalar", lambda e_, e=e: e_.copy(bc_sb[e][:], P1[:]),
                     reads=[P1], writes=[bc_sb[e]])
            if phases < 3:
                continue
            for e in range(NE):
                p.dma("sync", a1f[:], a1t[e], writes=[a1f])
                p.op("scalar", lambda e_: e_.copy(a1b[:], a1f[:]), reads=[a1f], writes=[a1b])
                for tt in range(4):
                    for di in range(DI):
                        p.op("tensor", lambda e_, tt=tt, di=di: e_.matmul(
                            P0[:, 0:16], xnT[:, di, tt * 128:(tt + 1) * 128], a1b[:, di, :],
                            start=(di == 0), stop=(di == DI - 1)),
                            reads=[xnT, a1b], writes=[P0])
                        pass
                    p.op("scalar", lambda e_: e_.activation(g_sb[:, 0:16], P0[:, 0:16],
                                                            AF.Identity, scale=2.0),
                         reads=[P0], writes=[g_sb])
                    p.op("tensor", lambda e_: e_.transpose(PB0[0:16, 0:128],
                                                           g_sb[:, 0:16], id_sb[:]),
                         reads=[g_sb, id_sb], writes=[PB0])
                    p.op("vector", lambda e_, e=e, tt=tt: e_.tensor_copy(
                        t1T[:, e, tt * 128:(tt + 1) * 128], PB0[0:16, 0:128]),
                        reads=[PB0], writes=[t1T])
            if phases < 4:
                continue
            p.op("vector", lambda e: e.memset(t2a[:], 0.0), writes=[t2a])
            for ft in range(NF):
                for di in range(DI):
                    bi = di % 2
                    p.dma("sync", w1st[bi][:], w1[ft, di], writes=[w1st[bi]])
                    p.op("scalar", lambda e, bi=bi: e.copy(w1b[bi][:], w1st[bi][:]),
                         reads=[w1st[bi]], writes=[w1b[bi]])
                    p.op("tensor", lambda e, bi=bi, di=di: e.matmul(
                        P0[:], w1b[bi][:], xnT[:, di, :],
                        start=(di == 0), stop=(di == DI - 1)),
                        reads=[w1b[bi], xnT], writes=[P0])
                p.dma("sync", b1col[:], b1c[ft], writes=[b1col])
                p.op("scalar", lambda e: e.activation(baseT[:], P0[:], AF.Identity,
                                                      bias=b1col[:]),
                     reads=[P0, b1col], writes=[baseT])
                if dbg and th == 0 and ft == 0:
                    p.dma("gpsimd", dbg_base, baseT[:], reads=[baseT])
                for e in range(NE):
                    bi = e % 2
                    p.dma("gpsimd", b1p[bi][:], b1tp[ft, e], writes=[b1p[bi]])
                    p.op("scalar", lambda e_, e=e, bi=bi: e_.copy(b1eb[:, e, :], b1p[bi][:]),
                         reads=[b1p[bi]], writes=[b1eb])
                for e in range(NE):
                    p.op("tensor", lambda e_: e_.matmul(P1[:], id_sb[:], baseT[:],
                                                        start=True, stop=False),
                         reads=[id_sb, baseT], writes=[P1])
                    p.op("tensor", lambda e_, e=e: e_.matmul(
                        P1[:], b1eb[:, e, :], t1T[:, e, :],
                        start=False, stop=True),
                        reads=[b1eb, t1T], writes=[P1])
                    p.op("scalar", lambda e_: e_.activation(g_sb[:], P1[:], GELU),
                         reads=[P1], writes=[g_sb])
                    if dbg and th == 0 and ft == 0 and e == 0:
                        p.dma("gpsimd", dbg_g, g_sb[:], reads=[g_sb])
                    p.op("vector", lambda e_, e=e: e_.tensor_mul(ca_sb[:], g_sb[:], bc_sb[e][:]),
                         reads=[g_sb, bc_sb[e]], writes=[ca_sb])
                    if dbg and th == 0 and ft == 0 and e == 0:
                        p.dma("gpsimd", dbg_ca, ca_sb[:], reads=[ca_sb])
                    p.op("tensor", lambda e_, e=e: e_.matmul(P2[:], id_sb[:], ca_sb[:],
                                                             start=(e == 0), stop=(e == NE - 1)),
                         reads=[id_sb, ca_sb], writes=[P2])
                    p.op("tensor", lambda e_, e=e, ft=ft: e_.matmul(
                        P3[0:16, :], a2_sb[:, e, ft, :], ca_sb[:],
                        start=True, stop=True),
                        reads=[a2_sb, ca_sb], writes=[P3])
                    p.op("vector", lambda e_, e=e: e_.tensor_add(
                        t2a[:, e, :], t2a[:, e, :], P3[0:16, :]),
                        reads=[P3, t2a], writes=[t2a])
                p.op("scalar", lambda e, ft=ft: e.copy(mixT[:, ft, :], P2[:]),
                     reads=[P2], writes=[mixT])
                if dbg and th == 0 and ft == 0:
                    p.dma("gpsimd", dbg_mix, mixT[:, 0, :], reads=[mixT])
            p.op("scalar", lambda e: e.copy(t2ab[:], t2a[:]), reads=[t2a], writes=[t2ab])
            if dbg and th == 0:
                p.dma("gpsimd", dbg_t2, t2ab[:], reads=[t2ab])
                p.dma("gpsimd", dbg_t1, t1T[:], reads=[t1T])
            if phases < 5:
                continue
            for dt in range(DI):
                for e in range(NE):
                    bi = e % 2
                    p.dma("gpsimd", b2st[bi][:], b2tp[e, :, dt * 128:(dt + 1) * 128],
                          writes=[b2st[bi]])
                    p.op("scalar", lambda e_, e=e, bi=bi: e_.activation(
                        b2tb[:, e, :], b2st[bi][:], AF.Identity, scale=2.0),
                        reads=[b2st[bi]], writes=[b2tb])
                for ft in range(NF):
                    bi = ft % 2
                    p.dma("sync", w2st[bi][:], w2[dt, ft], writes=[w2st[bi]])
                    p.op("scalar", lambda e, bi=bi: e.copy(w2b[bi][:], w2st[bi][:]),
                         reads=[w2st[bi]], writes=[w2b[bi]])
                    p.op("tensor", lambda e, bi=bi, ft=ft: e.matmul(
                        P0[:], w2b[bi][:], mixT[:, ft, :],
                        start=(ft == 0), stop=False),
                        reads=[w2b[bi], mixT], writes=[P0])
                if phases >= 6:
                    p.op("tensor", lambda e, dt=dt: e.matmul(
                        P0[:], b2bf[:, dt * 128:(dt + 1) * 128], csr[:],
                        start=False, stop=False),
                        reads=[b2bf, csr], writes=[P0])
                for e in range(NE if phases >= 7 else 0):
                    p.op("tensor", lambda e_, dt=dt, e=e: e_.matmul(
                        P0[:], b2tb[:, e, :],
                        t2ab[:, e, :],
                        start=False, stop=(e == NE - 1)),
                        reads=[b2tb, t2ab], writes=[P0])
                p.op("scalar", lambda e, dt=dt: e.copy(outT[:, dt, :], P0[:]),
                     reads=[P0], writes=[outT])
                if dbg and th == 0 and dt == 0:
                    p.dma("gpsimd", dbg_out, outT[:, 0, :], reads=[outT])
            if phases < 8:
                continue
            for tt in range(4):
                p.dma("sync", xt[:], xc[t0 + tt * 128:t0 + (tt + 1) * 128, :], writes=[xt])
                for dt in range(DI):
                    p.op("tensor", lambda e, dt=dt, tt=tt: e.transpose(
                        PB0[:, 0:128], outT[:, dt, tt * 128:(tt + 1) * 128], id_sb[:]),
                        reads=[outT, id_sb], writes=[PB0])
                    p.op("scalar", lambda e: e.copy(tcp[:], PB0[:, 0:128]),
                         reads=[PB0], writes=[tcp])
                    p.op("vector", lambda e, dt=dt: e.tensor_add(
                        yv[:, dt * 128:(dt + 1) * 128], xt[:, dt * 128:(dt + 1) * 128], tcp[:]),
                        reads=[xt, tcp], writes=[yv])
                p.dma("gpsimd", y[t0 + tt * 128:t0 + (tt + 1) * 128, :], yv[:], reads=[yv])

        p.emit(block)
    return nc


def launch_l0f(x, xnew, Wr, br_, W1, b1_, W2, b2_, A1, B1, A2, B2, phases=99, dbg=False):
    nc = build_l0f(phases, dbg)
    f32 = lambda a: np.asarray(a, np.float32)
    wr_r = np.ascontiguousarray(f32(Wr).reshape(DI, 128, NE).transpose(1, 0, 2))
    br_r = f32(br_).reshape(1, NE)
    w1_r = np.ascontiguousarray(f32(W1).reshape(DI, 128, NF, 128).transpose(2, 0, 1, 3))
    b1_r = np.ascontiguousarray(f32(b1_).reshape(NF, 128, 1))
    w2_r = np.ascontiguousarray(f32(W2).reshape(NF, 128, DI, 128).transpose(2, 0, 1, 3))
    b2_r = f32(b2_).reshape(1, D)
    a1_r = np.ascontiguousarray(f32(A1).transpose(0, 2, 1).reshape(NE, DI, 128, 16).transpose(0, 2, 1, 3))
    a2_r = np.ascontiguousarray(f32(A2).transpose(0, 2, 1).reshape(NE, NF, 128, 16).transpose(0, 2, 1, 3))
    b1_p = np.zeros((NF, NE, 16, 128), np.float32)
    b2_p = np.zeros((NE, 16, D), np.float32)
    for e in range(NE):
        b1t_e = f32(B1[e]).T.reshape(16, NF, 128)
        b1_p[:, e] = b1t_e.transpose(1, 0, 2)
        b2_p[e] = f32(B2[e]).T
    in_maps = []
    for c in range(8):
        b, r = c // 4, c % 4
        in_maps.append(dict(
            xc=np.ascontiguousarray(xnew[b][own_rows(r)]),
            wr=wr_r, br=br_r, w1=w1_r, b1c=b1_r, w2=w2_r, b2r=b2_r,
            a1t=a1_r, b1tp=b1_p, a2t=a2_r, b2tp=b2_p, ident=IDENT))
    outs = _run(nc, in_maps)
    out = np.array(xnew)
    for c in range(8):
        b, r = c // 4, c % 4
        out[b][own_rows(r)] = outs[c]["y"]
    if dbg:
        return out, outs
    return out



def build_l1a():
    nc = bass.Bass("TRN2", target_bir_lowering=False, debug=False)
    xb = nc.dram_tensor("xb", [SEQ, D], F32, kind="ExternalInput").ap()
    xq = nc.dram_tensor("xq", [NQ * 128, D], F32, kind="ExternalInput").ap()
    wq = nc.dram_tensor("wq", [16, DI, 128, 128], F32, kind="ExternalInput").ap()
    wk = nc.dram_tensor("wk", [16, DI, 128, 128], F32, kind="ExternalInput").ap()
    wv = nc.dram_tensor("wv", [16, DI, 128, 128], F32, kind="ExternalInput").ap()
    wo = nc.dram_tensor("wo", [DI, 128, D], F32, kind="ExternalInput").ap()
    lqk = nc.dram_tensor("lqk", [4, 1, 64], F32, kind="ExternalInput").ap()
    masks = nc.dram_tensor("masks", [128, 4, 128], BF16, kind="ExternalInput").ap()
    ident = nc.dram_tensor("ident", [128, 128], BF16, kind="ExternalInput").ap()
    y = nc.dram_tensor("y", [NQ * 128, D], F32, kind="ExternalOutput").ap()
    attn_dr = nc.dram_tensor("attn_dr", [16, 128, NQ * 128], BF16, kind="Internal").ap()
    xnTo_dr = nc.dram_tensor("xnTo_dr", [DI, 128, NQ * 128], BF16, kind="Internal").ap()

    CL = 1.0 - LAMBDA_INIT
    c1 = 1.0 / (128.0 * CL * CL)
    c2 = 1e-5 / (CL * CL)

    with ExitStack() as st:
        sb = lambda n, s, dt=F32: st.enter_context(nc.sbuf_tensor(n, s, dt))
        ps = lambda n, s, dt=F32: st.enter_context(nc.psum_tensor(n, s, dt))
        block = st.enter_context(nc.Block())
        p = Prog(nc, lambda n: st.enter_context(nc.semaphore(n)))

        xnT = sb("xnT", [128, DI, SEQ], BF16)
        tstg = sb("tstg", [128, 128], BF16)
        id_sb = sb("id_sb", [128, 128], BF16)
        mask_sb = sb("mask_sb", [128, 4, 128], BF16)
        xt = sb("xt", [128, D])
        xnbf = sb("xnbf", [128, D], BF16)
        yv = sb("yv", [128, D])
        tmp = mk_ln_tmp(st, nc, "ln", sq=yv)
        ptp = ps("ptp", [128, 128], BF16)

        p.dma("sync", id_sb[:], ident, writes=[id_sb])
        p.dma("sync", mask_sb[:], masks, writes=[mask_sb])

        lr = [sb(f"lr{i}", [1, 64]) for i in range(2)]
        lsc = sb("lsc", [1, 4])
        neglam = sb("neglam", [1, 1])
        neglam128 = sb("neglam128", [128, 1])
        onesr = sb("onesr", [1, 128])
        m1c = sb("m1c", [128, 1])
        p.op("vector", lambda e: e.memset(onesr[:], 1.0), writes=[onesr])
        p.op("vector", lambda e: e.memset(m1c[:], -M1), writes=[m1c])
        for i in range(2):
            p.dma("sync", lr[0][:], lqk[2 * i], writes=[lr[0]])
            p.dma("sync", lr[1][:], lqk[2 * i + 1], writes=[lr[1]])
            p.op("vector", lambda e: e.tensor_mul(lr[0][:], lr[0][:], lr[1][:]),
                 reads=[lr[0], lr[1]], writes=[lr[0]])
            p.op("vector", lambda e, i=i: e.reduce_sum(lsc[:, i:i + 1], lr[0][:], AX.X),
                 reads=[lr[0]], writes=[lsc])
        p.op("scalar", lambda e: e.activation(lsc[:, 2:4], lsc[:, 0:2], AF.Exp),
             reads=[lsc], writes=[lsc])
        p.op("vector", lambda e: e.tensor_sub(neglam[:], lsc[:, 3:4], lsc[:, 2:3]),
             reads=[lsc], writes=[neglam])
        p.op("vector", lambda e: e.tensor_scalar(neglam[:], neglam[:], -LAMBDA_INIT, None, ALU.add),
             reads=[neglam], writes=[neglam])

        for isq in (0, 1):
            ntile = NQ if isq else NT
            for kt in range(ntile):
                p.dma("sync", xt[:], (xq if isq else xb)[kt * 128:(kt + 1) * 128, :], writes=[xt])
                ln_tile(p, xt, xnbf, tmp)
                for di in range(DI):
                    p.op("tensor", lambda e, di=di: e.transpose(
                        ptp[:], xnbf[:, di * 128:(di + 1) * 128], id_sb[:]),
                        reads=[xnbf, id_sb], writes=[ptp])
                    if isq:
                        p.op("vector", lambda e: e.tensor_copy(tstg[:], ptp[:]),
                             reads=[ptp], writes=[tstg])
                        p.dma("gpsimd", xnTo_dr[di, :, kt * 128:(kt + 1) * 128], tstg[:],
                              reads=[tstg], writes=[xnTo_dr])
                    else:
                        p.op("vector", lambda e, di=di, kt=kt: e.tensor_copy(
                            xnT[:, di, kt * 128:(kt + 1) * 128], ptp[:]),
                            reads=[ptp], writes=[xnT])

        wst = sb("wst", [128, 128])
        wbf = sb("wbf", [128, DI, 128], BF16)
        kTp = sb("kTp", [128, SEQ], BF16)
        vN = sb("vN", [128, NT, 128], BF16)
        qTp = sb("qTp", [128, NQ * 128], BF16)
        qx = sb("qx", [128, DI, 128], BF16)
        vtb = sb("vtb", [128, 512], BF16)
        pt4 = sb("pt4", [128, 512], BF16)
        onesc = sb("onesc", [128, 1], BF16)
        sr = [sb(f"sr{i}", [1, 128]) for i in range(2)]
        o_sb = [sb(f"o_sb{i}", [128, 128]) for i in range(2)]
        rb = sb("rb", [128, 128])
        sq_sb = sb("sq_sb", [128, 128])
        onesf = sb("onesf", [128, 1])
        msr = sb("msr", [1, 128])
        aoT = sb("aoT", [128, 128], BF16)
        p.op("vector", lambda e: e.memset(onesc[:], 1.0), writes=[onesc])
        p.op("vector", lambda e: e.memset(onesf[:], 1.0), writes=[onesf])

        pk = ps("pk", [128, 512])
        psc = ps("psc", [128, 512])
        pav = [ps(f"pav{i}", [128, 128]) for i in range(2)]
        pss = [ps(f"pss{i}", [1, 128]) for i in range(2)]
        prb = ps("prb", [128, 128])

        p.op("tensor", lambda e: e.matmul(prb[:, 0:1], onesr[:], neglam[:],
                                          start=True, stop=True),
             reads=[onesr, neglam], writes=[prb])
        p.op("vector", lambda e: e.tensor_copy(neglam128[:], prb[:, 0:1]),
             reads=[prb], writes=[neglam128])

        def load_w(w, h):
            for di in range(DI):
                p.dma("sync", wst[:], w[h, di], writes=[wst])
                p.op("scalar", lambda e, di=di: e.copy(wbf[:, di, :], wst[:]),
                     reads=[wst], writes=[wbf])

        for h in range(16):
            load_w(wk, h)
            for ch in range(SEQ // 512):
                for di in range(DI):
                    p.op("tensor", lambda e, ch=ch, di=di: e.matmul(
                        pk[:], wbf[:, di, :], xnT[:, di, ch * 512:(ch + 1) * 512],
                        start=(di == 0), stop=(di == DI - 1)),
                        reads=[wbf, xnT], writes=[pk])
                p.op("scalar", lambda e, ch=ch: e.copy(
                    kTp[:, ch * 512:(ch + 1) * 512], pk[:]), reads=[pk], writes=[kTp])
            load_w(wv, h)
            for ch in range(SEQ // 512):
                for di in range(DI):
                    p.op("tensor", lambda e, ch=ch, di=di: e.matmul(
                        pk[:], wbf[:, di, :], xnT[:, di, ch * 512:(ch + 1) * 512],
                        start=(di == 0), stop=(di == DI - 1)),
                        reads=[wbf, xnT], writes=[pk])
                p.op("scalar", lambda e, ch=ch: e.copy(vtb[:], pk[:]), reads=[pk], writes=[vtb])
                for i in range(4):
                    kt = ch * 4 + i
                    p.op("tensor", lambda e, i=i: e.transpose(
                        ptp[:], vtb[:, i * 128:(i + 1) * 128], id_sb[:]),
                        reads=[vtb, id_sb], writes=[ptp])
                    p.op("vector", lambda e, kt=kt: e.tensor_copy(vN[:, kt, :], ptp[:]),
                         reads=[ptp], writes=[vN])
            load_w(wq, h)
            for j in range(NQ):
                for di in range(DI):
                    p.dma("sync", qx[:, di, :], xnTo_dr[di, :, j * 128:(j + 1) * 128],
                          reads=[xnTo_dr], writes=[qx])
                for di in range(DI):
                    p.op("tensor", lambda e, j=j, di=di: e.matmul(
                        psc[:, 0:128], wbf[:, di, :], qx[:, di, :],
                        start=(di == 0), stop=(di == DI - 1)),
                        reads=[wbf, qx], writes=[psc])
                p.op("scalar", lambda e, j=j: e.copy(
                    qTp[:, j * 128:(j + 1) * 128], psc[:, 0:128]), reads=[psc], writes=[qTp])
            for j in range(NQ):
                ng = j + 1
                nacc = 4 * ng
                for sh in (0, 1):
                    b0 = 64 * sh
                    acc = 0
                    for g in range(ng):
                        for i in range(4):
                            kt = g * 4 + i
                            p.op("tensor", lambda e, kt=kt, i=i, j=j, b0=b0: e.matmul(
                                psc[:, i * 128:(i + 1) * 128],
                                kTp[b0:b0 + 64, kt * 128:(kt + 1) * 128],
                                qTp[b0:b0 + 64, j * 128:(j + 1) * 128],
                                start=True, stop=True),
                                reads=[kTp, qTp], writes=[psc])
                        p.op("scalar", lambda e: e.activation(
                            pt4[:], psc[:], AF.Exp, bias=m1c[:], scale=0.125),
                            reads=[psc, m1c], writes=[pt4])
                        if g == ng - 1:
                            p.op("vector", lambda e: e.tensor_mul(
                                pt4[:], pt4[:], mask_sb[:].rearrange("p a b -> p (a b)")),
                                reads=[pt4, mask_sb], writes=[pt4])
                        for i in range(4):
                            kt = g * 4 + i
                            acc += 1
                            p.op("tensor", lambda e, i=i, acc=acc, sh=sh: e.matmul(
                                pss[sh][:], onesc[:], pt4[:, i * 128:(i + 1) * 128],
                                start=(acc == 1), stop=(acc == nacc)),
                                reads=[onesc, pt4], writes=[pss[sh]])
                            p.op("tensor", lambda e, kt=kt, i=i, acc=acc, sh=sh: e.matmul(
                                pav[sh][:], vN[:, kt, :], pt4[:, i * 128:(i + 1) * 128],
                                start=(acc == 1), stop=(acc == nacc)),
                                reads=[vN, pt4], writes=[pav[sh]])
                for sh in (0, 1):
                    p.op("vector", lambda e, sh=sh: e.tensor_copy(sr[sh][:], pss[sh][:]),
                         reads=[pss[sh]], writes=[sr[sh]])
                    p.op("vector", lambda e, sh=sh: e.reciprocal(sr[sh][:], sr[sh][:]),
                         reads=[sr[sh]], writes=[sr[sh]])
                    p.op("tensor", lambda e, sh=sh: e.matmul(prb[:], onesr[:], sr[sh][:],
                                                             start=True, stop=True),
                         reads=[onesr, sr[sh]], writes=[prb])
                    p.op("vector", lambda e: e.tensor_copy(rb[:], prb[:]),
                         reads=[prb], writes=[rb])
                    p.op("vector", lambda e, sh=sh: e.tensor_mul(o_sb[sh][:], pav[sh][:], rb[:]),
                         reads=[pav[sh], rb], writes=[o_sb[sh]])
                p.op("vector", lambda e: e.scalar_tensor_tensor(
                    o_sb[0][:], o_sb[1][:], neglam128[:], o_sb[0][:], ALU.mult, ALU.add),
                    reads=[o_sb[1], neglam128, o_sb[0]], writes=[o_sb[0]])
                p.op("scalar", lambda e: e.activation(sq_sb[:], o_sb[0][:], AF.Square),
                     reads=[o_sb[0]], writes=[sq_sb])
                p.op("tensor", lambda e: e.matmul(pss[0][:], onesf[:], sq_sb[:],
                                                  start=True, stop=True),
                     reads=[onesf, sq_sb], writes=[pss[0]])
                p.op("vector", lambda e: e.tensor_copy(msr[:], pss[0][:]),
                     reads=[pss[0]], writes=[msr])
                p.op("vector", lambda e: e.tensor_scalar(msr[:], msr[:], c1, c2, ALU.mult, ALU.add),
                     reads=[msr], writes=[msr])
                p.op("vector", lambda e: e.reciprocal(msr[:], msr[:]), reads=[msr], writes=[msr])
                p.op("scalar", lambda e: e.activation(msr[:], msr[:], AF.Sqrt),
                     reads=[msr], writes=[msr])
                p.op("tensor", lambda e: e.matmul(prb[:], onesr[:], msr[:],
                                                  start=True, stop=True),
                     reads=[onesr, msr], writes=[prb])
                p.op("vector", lambda e: e.tensor_copy(rb[:], prb[:]), reads=[prb], writes=[rb])
                p.op("vector", lambda e: e.tensor_mul(aoT[:], o_sb[0][:], rb[:]),
                     reads=[o_sb[0], rb], writes=[aoT])
                p.dma("gpsimd", attn_dr[h, :, j * 128:(j + 1) * 128], aoT[:],
                      reads=[aoT], writes=[attn_dr])

        at_sb = wbf
        wos = sb("wos", [128, 512])
        wob1 = sb("wob1", [128, 512], BF16)
        xq_sb = xt
        pop = pk
        for j in range(NQ):
            for di in range(DI):
                p.dma("sync", at_sb[:, di, :], attn_dr[di, :, j * 128:(j + 1) * 128],
                      reads=[attn_dr], writes=[at_sb])
            p.dma("sync", xq_sb[:], xq[j * 128:(j + 1) * 128, :], writes=[xq_sb])
            for dc in range(4):
                for di in range(DI):
                    p.dma("sync", wos[:], wo[di, :, dc * 512:(dc + 1) * 512], writes=[wos])
                    p.op("scalar", lambda e: e.copy(wob1[:], wos[:]), reads=[wos], writes=[wob1])
                    p.op("tensor", lambda e, dc=dc, di=di: e.matmul(
                        pop[:], at_sb[:, di, :], wob1[:],
                        start=(di == 0), stop=(di == DI - 1)),
                        reads=[at_sb, wob1], writes=[pop])
                p.op("vector", lambda e, dc=dc: e.tensor_add(
                    yv[:, dc * 512:(dc + 1) * 512], pop[:],
                    xq_sb[:, dc * 512:(dc + 1) * 512]),
                    reads=[pop, xq_sb], writes=[yv])
            p.dma("gpsimd", y[j * 128:(j + 1) * 128, :], yv[:], reads=[yv])

        p.emit(block)
    return nc


def launch_l1a(x, Wq, Wk, Wv, Wo, lq1, lk1, lq2, lk2):
    nc = build_l1a()
    f32 = lambda a: np.asarray(a, np.float32)
    resh = lambda W: np.ascontiguousarray(
        f32(W).reshape(DI, 128, 16, 128).transpose(2, 0, 1, 3))
    wq_r, wk_r, wv_r = resh(Wq), resh(Wk), resh(Wv)
    wo_r = np.ascontiguousarray(f32(Wo).reshape(DI, 128, D))
    lqk_r = np.stack([f32(a).reshape(1, 64) for a in (lq1, lk1, lq2, lk2)])
    in_maps = []
    for c in range(8):
        b, r = c // 4, c % 4
        in_maps.append(dict(
            xb=np.ascontiguousarray(x[b]),
            xq=np.ascontiguousarray(x[b][own_rows(r)]),
            wq=wq_r, wk=wk_r, wv=wv_r, wo=wo_r, lqk=lqk_r,
            masks=make_masks(r), ident=IDENT))
    outs = _run(nc, in_maps)
    xnew = np.array(x)
    for c in range(8):
        b, r = c // 4, c % 4
        xnew[b][own_rows(r)] = outs[c]["y"]
    return xnew



def build_l1f():
    nc = bass.Bass("TRN2", target_bir_lowering=False, debug=False)
    xc = nc.dram_tensor("xc", [NQ * 128, D], F32, kind="ExternalInput").ap()
    qup = nc.dram_tensor("qup", [DI, 128, 8192], I32, kind="ExternalInput").ap()
    sup = nc.dram_tensor("sup", [DI, 128, 8192], F32, kind="ExternalInput").ap()
    bup = nc.dram_tensor("bup", [NF, 128, 1], F32, kind="ExternalInput").ap()
    la = nc.dram_tensor("la", [128, DI, 16], F32, kind="ExternalInput").ap()
    lb = nc.dram_tensor("lb", [16, 8192], F32, kind="ExternalInput").ap()
    qdn = nc.dram_tensor("qdn", [NF, 128, D], I32, kind="ExternalInput").ap()
    sdn = nc.dram_tensor("sdn", [NF, 128, D], F32, kind="ExternalInput").ap()
    bdn = nc.dram_tensor("bdn", [DI, 128, 1], F32, kind="ExternalInput").ap()
    da = nc.dram_tensor("da", [128, NF, 16], F32, kind="ExternalInput").ap()
    db = nc.dram_tensor("db", [16, D], F32, kind="ExternalInput").ap()
    ident = nc.dram_tensor("ident", [128, 128], BF16, kind="ExternalInput").ap()
    y = nc.dram_tensor("y", [NQ * 128, D], F32, kind="ExternalOutput").ap()

    with ExitStack() as st:
        sb = lambda n, s, dt=F32: st.enter_context(nc.sbuf_tensor(n, s, dt))
        ps = lambda n, s, dt=F32: st.enter_context(nc.psum_tensor(n, s, dt))
        block = st.enter_context(nc.Block())
        p = Prog(nc, lambda n: st.enter_context(nc.semaphore(n)))

        id_sb = sb("id_sb", [128, 128], BF16)
        xt = sb("xt", [128, D])
        xnbf = sb("xnbf", [128, D], BF16)
        yv = sb("yv", [128, D])
        tmp = mk_ln_tmp(st, nc, "ln", sq=yv)
        xnT = sb("xnT", [128, DI, 512], BF16)
        x2T = sb("x2T", [128, NF, 512], BF16)
        outTd = sb("outTd", [128, DI, 512], BF16)
        uT = sb("uT", [16, 512], BF16)
        vT = sb("vT", [16, 512], BF16)
        la_bf = sb("la_bf", [128, DI, 16], BF16)
        da_bf = sb("da_bf", [128, NF, 16], BF16)
        lb_bf = sb("lb_bf", [16, NF, 128], BF16)
        db_bf = sb("db_bf", [16, DI, 128], BF16)
        lst = sb("lst", [128, NF * 16])
        lrow = sb("lrow", [16, 1024])
        bcol = sb("bcol", [128, 1])
        qst = [sb(f"qst{i}", [128, 256], I32) for i in range(2)]
        sst = [sb(f"sst{i}", [128, 256]) for i in range(2)]
        qf = [sb(f"qf{i}", [128, 256]) for i in range(2)]
        wti = [sb(f"wti{i}", [128, 256], BF16) for i in range(2)]
        tcp = sb("tcp", [128, 128])
        tcb = sb("tcb", [128, 16], BF16)

        P0 = ps("P0", [128, 512])
        P1 = ps("P1", [128, 512])
        P2 = ps("P2", [128, 512])
        PB0 = ps("PB0", [128, 256], BF16)

        p.dma("sync", id_sb[:], ident, writes=[id_sb])
        p.dma("sync", lst[:, 0:DI * 16], la.rearrange("p d r -> p (d r)"), writes=[lst])
        p.op("scalar", lambda e: e.copy(la_bf[:].rearrange("p d r -> p (d r)"), lst[:, 0:DI * 16]),
             reads=[lst], writes=[la_bf])
        p.dma("sync", lst[:, 0:NF * 16], da.rearrange("p f r -> p (f r)"), writes=[lst])
        p.op("scalar", lambda e: e.copy(da_bf[:].rearrange("p f r -> p (f r)"), lst[:, 0:NF * 16]),
             reads=[lst], writes=[da_bf])
        for ch in range(8):
            p.dma("sync", lrow[:, 0:1024], lb[:, ch * 1024:(ch + 1) * 1024], writes=[lrow])
            p.op("scalar", lambda e, ch=ch: e.copy(
                lb_bf[:].rearrange("r f c -> r (f c)")[:, ch * 1024:(ch + 1) * 1024],
                lrow[:, 0:1024]), reads=[lrow], writes=[lb_bf])
        for ch in range(2):
            p.dma("sync", lrow[:, 0:1024], db[:, ch * 1024:(ch + 1) * 1024], writes=[lrow])
            p.op("scalar", lambda e, ch=ch: e.copy(
                db_bf[:].rearrange("r d c -> r (d c)")[:, ch * 1024:(ch + 1) * 1024],
                lrow[:, 0:1024]), reads=[lrow], writes=[db_bf])

        for th in range(2):
            t0 = th * 512
            for tt in range(4):
                p.dma("sync", xt[:], xc[t0 + tt * 128:t0 + (tt + 1) * 128, :], writes=[xt])
                ln_tile(p, xt, xnbf, tmp)
                for di in range(DI):
                    p.op("tensor", lambda e, di=di: e.transpose(
                        PB0[:, 0:128], xnbf[:, di * 128:(di + 1) * 128], id_sb[:]),
                        reads=[xnbf, id_sb], writes=[PB0])
                    p.op("vector", lambda e, di=di, tt=tt: e.tensor_copy(
                        xnT[:, di, tt * 128:(tt + 1) * 128], PB0[:, 0:128]),
                        reads=[PB0], writes=[xnT])
            for tt in range(4):
                for di in range(DI):
                    p.op("tensor", lambda e, tt=tt, di=di: e.matmul(
                        P0[:, 0:16], xnT[:, di, tt * 128:(tt + 1) * 128], la_bf[:, di, :],
                        start=(di == 0), stop=(di == DI - 1)),
                        reads=[xnT, la_bf], writes=[P0])
                p.op("scalar", lambda e: e.copy(tcb[:], P0[:, 0:16]), reads=[P0], writes=[tcb])
                p.op("tensor", lambda e: e.transpose(PB0[0:16, 0:128], tcb[:], id_sb[:]),
                     reads=[tcb, id_sb], writes=[PB0])
                p.op("vector", lambda e, tt=tt: e.tensor_copy(
                    uT[:, tt * 128:(tt + 1) * 128], PB0[0:16, 0:128]),
                    reads=[PB0], writes=[uT])
            for ft in range(NF):
                for di in range(DI):
                    bi = di % 2
                    p.dma("sync", qst[bi][:, 0:128], qup[di, :, ft * 128:(ft + 1) * 128],
                          writes=[qst[bi]])
                    p.dma("gpsimd", sst[bi][:, 0:128], sup[di, :, ft * 128:(ft + 1) * 128],
                          writes=[sst[bi]])
                    p.op("scalar", lambda e, bi=bi: e.activation(
                        qf[bi][:, 0:128], qst[bi][:, 0:128], AF.Copy, bias=-7.5),
                        reads=[qst[bi]], writes=[qf[bi]])
                    p.op("vector", lambda e, bi=bi: e.tensor_mul(
                        wti[bi][:, 0:128], qf[bi][:, 0:128], sst[bi][:, 0:128]),
                        reads=[qf[bi], sst[bi]], writes=[wti[bi]])
                    p.op("tensor", lambda e, bi=bi, di=di: e.matmul(
                        P0[:], wti[bi][:, 0:128], xnT[:, di, :],
                        start=(di == 0), stop=False),
                        reads=[wti[bi], xnT], writes=[P0])
                p.op("tensor", lambda e, ft=ft: e.matmul(
                    P0[:], lb_bf[:, ft, :], uT[:], start=False, stop=True),
                    reads=[lb_bf, uT], writes=[P0])
                p.dma("sync", bcol[:], bup[ft], writes=[bcol])
                p.op("scalar", lambda e, ft=ft: e.activation(
                    x2T[:, ft, :], P0[:], AF.Relu, bias=bcol[:]),
                    reads=[P0, bcol], writes=[x2T])
            for tt in range(4):
                for ft in range(NF):
                    p.op("tensor", lambda e, tt=tt, ft=ft: e.matmul(
                        P0[:, 0:16], x2T[:, ft, tt * 128:(tt + 1) * 128], da_bf[:, ft, :],
                        start=(ft == 0), stop=(ft == NF - 1)),
                        reads=[x2T, da_bf], writes=[P0])
                p.op("scalar", lambda e: e.copy(tcb[:], P0[:, 0:16]), reads=[P0], writes=[tcb])
                p.op("tensor", lambda e: e.transpose(PB0[0:16, 0:128], tcb[:], id_sb[:]),
                     reads=[tcb, id_sb], writes=[PB0])
                p.op("vector", lambda e, tt=tt: e.tensor_copy(
                    vT[:, tt * 128:(tt + 1) * 128], PB0[0:16, 0:128]),
                    reads=[PB0], writes=[vT])
            for dt in range(DI):
                for ft in range(NF):
                    bi = ft % 2
                    p.dma("sync", qst[bi][:, 0:128], qdn[ft, :, dt * 128:(dt + 1) * 128],
                          writes=[qst[bi]])
                    p.dma("gpsimd", sst[bi][:, 0:128], sdn[ft, :, dt * 128:(dt + 1) * 128],
                          writes=[sst[bi]])
                    p.op("scalar", lambda e, bi=bi: e.activation(
                        qf[bi][:, 0:128], qst[bi][:, 0:128], AF.Copy, bias=-7.5),
                        reads=[qst[bi]], writes=[qf[bi]])
                    p.op("vector", lambda e, bi=bi: e.tensor_mul(
                        wti[bi][:, 0:128], qf[bi][:, 0:128], sst[bi][:, 0:128]),
                        reads=[qf[bi], sst[bi]], writes=[wti[bi]])
                    p.op("tensor", lambda e, bi=bi, ft=ft: e.matmul(
                        P1[:], wti[bi][:, 0:128], x2T[:, ft, :],
                        start=(ft == 0), stop=False),
                        reads=[wti[bi], x2T], writes=[P1])
                p.op("tensor", lambda e, dt=dt: e.matmul(
                    P1[:], db_bf[:, dt, :], vT[:], start=False, stop=True),
                    reads=[db_bf, vT], writes=[P1])
                p.dma("sync", bcol[:], bdn[dt], writes=[bcol])
                p.op("scalar", lambda e, dt=dt: e.activation(
                    outTd[:, dt, :], P1[:], AF.Identity, bias=bcol[:]),
                    reads=[P1, bcol], writes=[outTd])
            for tt in range(4):
                p.dma("sync", xt[:], xc[t0 + tt * 128:t0 + (tt + 1) * 128, :], writes=[xt])
                for dt in range(DI):
                    p.op("tensor", lambda e, dt=dt, tt=tt: e.transpose(
                        PB0[:, 0:128], outTd[:, dt, tt * 128:(tt + 1) * 128], id_sb[:]),
                        reads=[outTd, id_sb], writes=[PB0])
                    p.op("scalar", lambda e: e.copy(tcp[:], PB0[:, 0:128]),
                         reads=[PB0], writes=[tcp])
                    p.op("vector", lambda e, dt=dt: e.tensor_add(
                        yv[:, dt * 128:(dt + 1) * 128], xt[:, dt * 128:(dt + 1) * 128], tcp[:]),
                        reads=[xt, tcp], writes=[yv])
                p.dma("gpsimd", y[t0 + tt * 128:t0 + (tt + 1) * 128, :], yv[:], reads=[yv])

        p.emit(block)
    return nc


def launch_l1f(xnew, w_up_q, w_up_scale, b_up, w_up_la, w_up_lb,
               w_down_q, w_down_scale, b_down, w_down_la, w_down_lb):
    nc = build_l1f()
    f32 = lambda a: np.asarray(a, np.float32)
    qup_r = np.ascontiguousarray(np.asarray(w_up_q, np.int32).T.reshape(DI, 128, 8192))
    sup_r = np.ascontiguousarray(
        np.repeat(f32(w_up_scale), 64, axis=1).T.reshape(DI, 128, 8192))
    bup_r = np.ascontiguousarray(f32(b_up).reshape(NF, 128, 1))
    la_r = np.ascontiguousarray(f32(w_up_la).reshape(DI, 128, 16).transpose(1, 0, 2))
    lb_r = np.ascontiguousarray(f32(w_up_lb))
    qdn_r = np.ascontiguousarray(np.asarray(w_down_q, np.int32).T.reshape(NF, 128, D))
    sdn_r = np.ascontiguousarray(
        np.repeat(f32(w_down_scale), 64, axis=1).T.reshape(NF, 128, D))
    bdn_r = np.ascontiguousarray(f32(b_down).reshape(DI, 128, 1))
    da_r = np.ascontiguousarray(f32(w_down_la).reshape(NF, 128, 16).transpose(1, 0, 2))
    db_r = np.ascontiguousarray(f32(w_down_lb))
    in_maps = []
    for c in range(8):
        b, r = c // 4, c % 4
        in_maps.append(dict(
            xc=np.ascontiguousarray(xnew[b][own_rows(r)]),
            qup=qup_r, sup=sup_r, bup=bup_r, la=la_r, lb=lb_r,
            qdn=qdn_r, sdn=sdn_r, bdn=bdn_r, da=da_r, db=db_r, ident=IDENT))
    outs = _run(nc, in_maps)
    out = np.array(xnew)
    for c in range(8):
        b, r = c // 4, c % 4
        out[b][own_rows(r)] = outs[c]["y"]
    return out



def make_masks(r):
    kk = np.arange(128)[:, None, None]
    ii = np.arange(4)[None, :, None]
    qq = np.arange(128)[None, None, :]
    m = (ii * 128 + kk <= r * 128 + qq).astype(np.float32)
    return m.astype(ml_dtypes.bfloat16)


IDENT = np.eye(128).astype(ml_dtypes.bfloat16)


def own_rows(r):
    idx = []
    for j in range(NQ):
        t0 = (r + 4 * j) * 128
        idx.extend(range(t0, t0 + 128))
    return np.asarray(idx)


def launch_l0a(x, Wqkv, bqkv_, Wo, bo_):
    nc = build_l0a()
    Wqkv = np.asarray(Wqkv, np.float32)
    wq = np.ascontiguousarray(
        Wqkv.reshape(DI, 128, 3, 16, 128).transpose(3, 2, 0, 1, 4))
    bq = np.ascontiguousarray(
        np.asarray(bqkv_, np.float32).reshape(3, 16, 128).transpose(1, 0, 2)[:, :, :, None])
    wor = np.ascontiguousarray(np.asarray(Wo, np.float32).reshape(DI, 128, D))
    bor = np.asarray(bo_, np.float32).reshape(1, D)
    in_maps = []
    for c in range(8):
        b, r = c // 4, c % 4
        in_maps.append(dict(
            xb=np.ascontiguousarray(x[b]),
            xq=np.ascontiguousarray(x[b][own_rows(r)]),
            wqkv=wq, bqkv=bq, wo=wor, bo=bor,
            masks=make_masks(r), ident=IDENT))
    outs = _run(nc, in_maps)
    xnew = np.array(x)
    for c in range(8):
        b, r = c // 4, c % 4
        xnew[b][own_rows(r)] = outs[c]["y"]
    return xnew


def kernel(**inputs):
    i = inputs
    x = np.asarray(i["L0_attn_x"], np.float32)
    x = launch_l0a(x, i["L0_attn_Wqkv"], i["L0_attn_bqkv"], i["L0_attn_Wo"], i["L0_attn_bo"])
    x = launch_l0f(None, x, i["L0_ffn_Wr"], i["L0_ffn_br"], i["L0_ffn_W1"], i["L0_ffn_b1"],
                   i["L0_ffn_W2"], i["L0_ffn_b2"], i["L0_ffn_A1"], i["L0_ffn_B1"],
                   i["L0_ffn_A2"], i["L0_ffn_B2"])
    x = launch_l1a(x, i["L1_attn_Wq"], i["L1_attn_Wk"], i["L1_attn_Wv"], i["L1_attn_Wo"],
                   i["L1_attn_lq1"], i["L1_attn_lk1"], i["L1_attn_lq2"], i["L1_attn_lk2"])
    x = launch_l1f(x, i["L1_ffn_w_up_q"], i["L1_ffn_w_up_scale"], i["L1_ffn_b_up"],
                   i["L1_ffn_w_up_lora_a"], i["L1_ffn_w_up_lora_b"], i["L1_ffn_w_down_q"],
                   i["L1_ffn_w_down_scale"], i["L1_ffn_b_down"], i["L1_ffn_w_down_lora_a"],
                   i["L1_ffn_w_down_lora_b"])
    return np.ascontiguousarray(x.astype(np.float32))
